# version 18
# baseline (speedup 1.0000x reference)
import numpy as np
import ml_dtypes
import concourse.bass as bass
import concourse.mybir as mybir
from concourse.bass_utils import run_bass_kernel_spmd
from contextlib import ExitStack

F32 = mybir.dt.float32
BF16 = mybir.dt.bfloat16
AF = mybir.ActivationFunctionType
ALU = mybir.AluOpType
AX = mybir.AxisListType

D = 2048
CT = 16
TSC = 1024
NSC = 2
HALO = 256
KV = HALO + TSC
NKVT = KV // 128
NQT = TSC // 128
NFT = 11
E, FMOE, FSH = 8, 1408, 5632
SCALE = 1.0 / float(np.sqrt(128.0))
NEG = -1.0e30
EPS = 1.0e-5
FC_SLICES = [256, 256, 256, 256, 256, 128]
ENGS = ("sync", "gpsimd", "vector", "scalar", "tensor")
ESEM = {"sync": "sS", "gpsimd": "sG", "vector": "sV", "scalar": "sA", "tensor": "sT"}


SEMS = ("sS", "sG", "sV", "sA", "sT", "sW0", "sW1", "sP0", "sP1", "sGT",
        "sX", "sXA", "sM", "sR", "sC", "sI", "sO", "sD")


class Plan:
    def __init__(self):
        self.steps = {e: [] for e in ENGS}
        self.cnt = {s: 0 for s in SEMS}

    def step(self, eng, waits, fn, kind="drain", sem=None):
        if sem is None:
            sem = ESEM[eng]
        if kind == "dma":
            self.cnt[sem] += 16
        elif kind in ("last", "drain"):
            self.cnt[sem] += 1
        ev = (sem, self.cnt[sem]) if kind != "none" else None
        ws = tuple((s, v) for (s, v) in waits if v > 0)
        self.steps[eng].append((ws, fn, kind, sem, self.cnt[sem]))
        return ev

    def now(self, sem):
        return (sem, self.cnt[sem])

    def run(self, name, e, sems):
        waited = {}
        for ws, fn, kind, sem, _ in self.steps[name]:
            for (s, v) in ws:
                if v > waited.get(s, 0):
                    e.wait_ge(sems[s], v)
                    waited[s] = v
            ins = fn(e)
            if kind == "dma":
                ins.then_inc(sems[sem], 16)
            elif kind == "last":
                ins.then_inc(sems[sem], 1)
            elif kind == "drain":
                e.drain().then_inc(sems[sem], 1)


class Banks:
    def __init__(self):
        self.free = {}
        self.wrote = {}

    def wf(self, b):
        ev = self.free.get(b)
        return [ev] if ev else []

    def ww(self, b):
        return [self.wrote[b]]


def build(dbg=0, n_sc=NSC):
    nc = bass.Bass("TRN2", target_bir_lowering=False)
    ap = lambda t: t.ap()

    x_own = ap(nc.dram_tensor("x_own", [2048, D], F32, kind="ExternalInput"))
    x_halo = ap(nc.dram_tensor("x_halo", [HALO, D], F32, kind="ExternalInput"))
    x_sink = ap(nc.dram_tensor("x_sink", [4, D], F32, kind="ExternalInput"))
    wqT = ap(nc.dram_tensor("wqT", [D, 2048], F32, kind="ExternalInput"))
    wkT = ap(nc.dram_tensor("wkT", [D, 512], F32, kind="ExternalInput"))
    wvT = ap(nc.dram_tensor("wvT", [D, 512], F32, kind="ExternalInput"))
    woT = ap(nc.dram_tensor("woT", [2048, D], F32, kind="ExternalInput"))
    gTh = ap(nc.dram_tensor("gateT", [D, 9], F32, kind="ExternalInput"))
    fc1T = ap(nc.dram_tensor("efc1T", [E, D, FMOE], F32, kind="ExternalInput"))
    fc2T = ap(nc.dram_tensor("efc2T", [E, D, FMOE], F32, kind="ExternalInput"))
    prjT = ap(nc.dram_tensor("eprjT", [E, FMOE, D], F32, kind="ExternalInput"))
    sf1T = ap(nc.dram_tensor("sfc1T", [D, FSH], F32, kind="ExternalInput"))
    sf2T = ap(nc.dram_tensor("sfc2T", [D, FSH], F32, kind="ExternalInput"))
    sprT = ap(nc.dram_tensor("sprjT", [FSH, D], F32, kind="ExternalInput"))
    masks = ap(nc.dram_tensor("masks", [NSC, 4, 128, 512], F32, kind="ExternalInput"))
    out = ap(nc.dram_tensor("out", [2048, D], F32, kind="ExternalOutput"))

    cmb_scr = ap(nc.dram_tensor("cmb_scr", [E + 1, TSC], F32, kind="Internal"))

    idf_h = ap(nc.inline_tensor(np.eye(128, dtype=np.float32), name="idf_h"))
    idb_h = ap(nc.inline_tensor(np.eye(128, dtype=ml_dtypes.bfloat16), name="idb_h"))

    dbg_outs = {}

    with ExitStack() as st:
        en = st.enter_context
        S = lambda name, shape, dt: en(nc.sbuf_tensor(name, shape, dt))[:]
        xnkv = S("xnkv", [128, CT * KV], BF16)
        acc = S("acc", [128, CT * TSC], F32)
        qao = S("qao", [128, CT * TSC], BF16)
        hT = qao[:, 0:NFT * TSC]
        kT = S("kT", [128, 4 * KV], BF16)
        vS = S("vS", [128, NKVT * 512], BF16)
        wbuf = S("wbuf", [128, 2 * 4096], BF16)
        xio = S("xio", [128, 2048], F32)
        bscr = S("bscr", [128, 2048], BF16)
        maskS = S("maskS", [128, 512], F32)
        cmbrep = S("cmbrep", [128, TSC], F32)
        A = cmbrep
        pbuf = S("pbuf", [128, 2 * 1408], BF16)
        h1s = S("h1s", [128, 2 * 512], F32)
        h2c = S("h2c", [128, 2 * 512], F32)
        xns = S("xns", [128, CT * 4], BF16)
        ksT = S("ksT", [128, 4 * 4], BF16)
        vsS = S("vsS", [4, 512], BF16)
        gateT = S("gateTs", [128, CT * 9], BF16)
        stats = S("stats", [128, 16], F32)
        pr = S("pr", [128, 24], F32)
        consts = S("consts", [128, 2], F32)
        idf = S("idf", [128, 128], F32)
        idb = S("idb", [128, 128], BF16)

        cmbT = cmbrep[0:8, :]
        sgrow = cmbrep[32:33, 0:1024]
        xnkv3 = xnkv.rearrange("p (t r) -> p t r", t=CT)
        xns3 = xns.rearrange("p (t r) -> p t r", t=CT)
        acc3 = acc.rearrange("p (t r) -> p t r", t=CT)

        import os as _os
        if _os.environ.get("KPRINT"):
            print("sbuf_bytes_remaining:", nc.sbuf_bytes_remaining)
        assert nc.sbuf_bytes_remaining >= 0, f"SBUF over by {-nc.sbuf_bytes_remaining}"

        P = lambda name, shape, dt: en(nc.psum_tensor(name, shape, dt))[:]
        ps01 = P("ps01", [128, 1024], F32)
        ps23 = P("ps23", [128, 1024], F32)
        pb4 = P("pb4", [128, 1024], BF16)
        pb5 = P("pb5", [128, 1024], BF16)
        ps67 = P("ps67", [128, 1024], F32)
        fbank = {0: ps01[:, 0:512], 1: ps01[:, 512:1024], 2: ps23[:, 0:512],
                 3: ps23[:, 512:1024], 6: ps67[:, 0:512], 7: ps67[:, 512:1024],
                 4: pb4.bitcast(F32), 5: pb5.bitcast(F32)}

        sems = {}
        for sn in SEMS:
            sems[sn] = en(nc.semaphore(sn))
        block = en(nc.Block())

        plan = Plan()
        banks = Banks()
        dbg_guard = {"ev": None}

        def init_consts(v):
            v.memset(consts[:, 0:1], EPS)
            v.drain()
            v.memset(consts[:, 1:2], SCALE)
            v.drain()
            v.memset(pb4.bitcast(F32)[:, :], 0.0)
            v.drain()
            v.memset(pb5.bitcast(F32)[:, :], 0.0)
        ev_consts = plan.step("vector", [], init_consts)
        banks.free[4] = ev_consts
        banks.free[5] = ev_consts
        ev_idf = plan.step("sync", [], lambda g: g.dma_start(out=idf, in_=idf_h),
                           kind="dma", sem="sI")
        ev_idb = plan.step("sync", [], lambda g: g.dma_start(out=idb, in_=idb_h),
                           kind="dma", sem="sI")
        wq_state = {"i": 0, "done": {}}

        def wslice(src2d, width, nt):
            i = wq_state["i"]
            slot = i % 2
            waits = [wq_state["done"][i - 2]] if i >= 2 else []
            src3 = src2d.rearrange("(t p) f -> p t f", p=128)
            dst3 = wbuf[:, slot * 4096: slot * 4096 + nt * width] \
                .rearrange("p (t f) -> p t f", t=nt)
            ev = plan.step("gpsimd", waits,
                           (lambda g, _d=dst3, _s=src3: g.dma_start(out=_d, in_=_s)),
                           kind="dma", sem=("sW0" if slot == 0 else "sW1"))
            wq_state["i"] = i + 1
            return slot, ev, i

        def wdone(idx, ev):
            wq_state["done"][idx] = ev

        def wtile(slot, ct, w, ftl):
            o = slot * 4096 + ct * w + ftl * 128
            return wbuf[:, o: o + 128]

        def chain(bank_ap, bank, pairs, extra_waits):
            waits = list(extra_waits) + banks.wf(bank)
            def fn(t, _pairs=tuple(pairs), _out=bank_ap):
                last = None
                n = len(_pairs)
                for i, (l, r) in enumerate(_pairs):
                    last = t.matmul(_out, l, r, start=(i == 0), stop=(i == n - 1))
                return last
            ev = plan.step("tensor", waits, fn, kind="last")
            banks.wrote[bank] = ev
            return ev

        def evacuate(bank, eng, fn, extra_waits=()):
            waits = list(extra_waits) + banks.ww(bank)
            if dbg_guard["ev"]:
                waits.append(dbg_guard["ev"])
            ev = plan.step(eng, waits, fn, kind="drain")
            banks.free[bank] = ev
            return ev

        def tbatch(bank, pairs, extra_waits, ident):
            waits = list(extra_waits) + banks.wf(bank) + [ev_idb, ev_idf]
            def fn(t, _pairs=tuple(pairs), _id=ident):
                last = None
                for (o, i) in _pairs:
                    last = t.transpose(o, i, _id)
                return last
            ev = plan.step("tensor", waits, fn, kind="last")
            banks.wrote[bank] = ev
            return ev

        def dump(name, src_ap, waits):
            if not dbg:
                return None
            if name in dbg_outs:
                d = dbg_outs[name]
            else:
                d = ap(nc.dram_tensor("dbg_" + name, list(src_ap.shape), src_ap.dtype,
                                      kind="ExternalOutput"))
                dbg_outs[name] = d
            return plan.step("sync", waits,
                             (lambda g, _s=src_ap, _d=d: g.dma_start(out=_d, in_=_s)),
                             kind="dma", sem="sD")

        last_out_ev = None
        attn_done_v = attn_done_a = 0
        xio_readers = []
        prev_bscr_readers = []

        for sc in range(n_sc):
            g0 = sc * TSC
            if dbg and sc > 0:
                dbg_guard["ev"] = plan.now("sD")

            xn_evts = []
            xn_chunk_evts = [[], []]
            for seq_t in range(11):
                tt = seq_t + 2 if seq_t < 8 else (seq_t - 8 if seq_t < 10 else 10)
                if tt < 2:
                    src = x_halo[tt * 128:(tt + 1) * 128, :] if sc == 0 else \
                          x_own[TSC - HALO + tt * 128: TSC - HALO + (tt + 1) * 128, :]
                    Pn, dstc = 128, tt * 128
                elif tt < 10:
                    src = x_own[g0 + (tt - 2) * 128: g0 + (tt - 1) * 128, :]
                    Pn, dstc = 128, HALO + (tt - 2) * 128
                else:
                    src, Pn, dstc = x_sink[0:4, :], 4, 0

                lw = list(xio_readers)
                if sc > 0:
                    lw += [("sV", attn_done_v), ("sA", attn_done_a)]
                ld = plan.step("sync", lw,
                               (lambda g, _s=src, _p=Pn: g.dma_start(out=xio[0:_p, :], in_=_s)),
                               kind="dma", sem="sX")
                xio_readers = []

                def v1(v, p=Pn):
                    v.reduce_sum(stats[0:p, 0:1], xio[0:p, :], axis=AX.X)
                    v.drain()
                    v.tensor_scalar_mul(stats[0:p, 1:2], stats[0:p, 0:1], 1.0 / D)
                    v.drain()
                    v.tensor_mul(stats[0:p, 2:3], stats[0:p, 1:2], stats[0:p, 1:2])
                ev_v1 = plan.step("vector", [ld], v1)

                def s1(s, p=Pn):
                    s.activation(bscr[0:p, :], xio[0:p, :], AF.Square,
                                 accum_out=stats[0:p, 3:4])
                ev_s1 = plan.step("scalar", [ld, ev_consts] + prev_bscr_readers, s1)

                def v2(v, p=Pn):
                    v.tensor_scalar(stats[0:p, 4:5], stats[0:p, 3:4], 1.0 / D,
                                    stats[0:p, 2:3], ALU.mult, ALU.subtract)
                ev_v2 = plan.step("vector", [ev_s1, ev_v1], v2)

                def s2(s, p=Pn):
                    s.activation(stats[0:p, 5:6], stats[0:p, 4:5], AF.Sqrt,
                                 bias=consts[0:p, 0:1])
                ev_s2 = plan.step("scalar", [ev_v2], s2)

                def v3(v, p=Pn):
                    v.reciprocal(stats[0:p, 6:7], stats[0:p, 5:6])
                    v.drain()
                    v.tensor_scalar(bscr[0:p, :], xio[0:p, :], stats[0:p, 1:2],
                                    stats[0:p, 6:7], ALU.subtract, ALU.mult)
                ev_v3 = plan.step("vector", [ev_s2] + prev_bscr_readers, v3)
                xio_readers += [ev_v3, ev_s1]
                prev_bscr_readers = []

                ident = idb if Pn == 128 else idb[0:4, 0:4]
                for half in range(2):
                    pb, bk = (pb4, 4) if half == 0 else (pb5, 5)
                    prs = [(pb[:, i * 128: i * 128 + Pn],
                            bscr[0:Pn, (half * 8 + i) * 128: (half * 8 + i + 1) * 128])
                           for i in range(8)]
                    tev = tbatch(bk, prs, [ev_v3], ident)
                    prev_bscr_readers.append(tev)
                    if tt < 10:
                        def ev_fn(v, h=half, _pb=pb, c=dstc):
                            d = xnkv3[:, h * 8:(h + 1) * 8, c:c + 128]
                            s_ = _pb.rearrange("q (i r) -> q i r", i=8)
                            v.tensor_copy(d, s_)
                    else:
                        def ev_fn(v, h=half, _pb=pb):
                            d = xns3[:, h * 8:(h + 1) * 8, :]
                            s_ = _pb.rearrange("q (i r) -> q i r", i=8)[:, :, 0:4]
                            v.tensor_copy(d, s_)
                    xev = evacuate(bk, "vector", ev_fn)
                    xn_evts.append(xev)
                    if 2 <= tt < 10:
                        xn_chunk_evts[(tt - 2) // 4].append(xev)


            ev_xn_all = ("sV", max(v for (_, v) in xn_evts))
            ev_xn_ch = [("sV", max(v for (_, v) in xn_chunk_evts[c])) for c in range(2)]

            if sc == 0:
                g_ev = plan.step("gpsimd", [], lambda g: g.dma_start(
                    out=gateT.rearrange("p (t f) -> p t f", t=CT),
                    in_=gTh.rearrange("(t p) f -> p t f", p=128)), kind="dma", sem="sGT")
                sg_ev = g_ev

            def own_rhs(ct, ch):
                o = ct * KV + HALO + ch * 512
                return xnkv[:, o: o + 512]

            KVROT = [0, 1, 2]
            kvi = [0]

            def kv_bk():
                b = KVROT[kvi[0] % 3]
                kvi[0] += 1
                return b

            kchunks = [(0, 256), (256, 512), (768, 512)]
            for s2 in range(2):
                slot, rdy, idx = wslice(wkT[:, s2 * 256:(s2 + 1) * 256], 256, CT)
                last_ev = None
                for ftl in range(2):
                    kvh = 2 * s2 + ftl
                    for ci, (off, ln) in enumerate(kchunks):
                        bk = kv_bk()
                        prs = [(wtile(slot, ct, 256, ftl), xnkv[:, ct * KV + off: ct * KV + off + ln])
                               for ct in range(CT)]
                        chain(fbank[bk][:, 0:ln], bk, prs, [rdy, ev_xn_all])
                        evacuate(bk, "scalar",
                                 (lambda s, _b=bk, _k=kvh, _o=off, _l=ln:
                                  s.copy(kT[:, _k * KV + _o: _k * KV + _o + _l],
                                         fbank[_b][:, 0:_l])))
                    bk = kv_bk()
                    prs = [(wtile(slot, ct, 256, ftl), xns[:, ct * 4: ct * 4 + 4])
                           for ct in range(CT)]
                    last_ev = chain(fbank[bk][:, 0:4], bk, prs, [rdy, ev_xn_all])
                    evacuate(bk, "scalar",
                             (lambda s, _b=bk, _k=kvh:
                              s.copy(ksT[:, _k * 4: _k * 4 + 4], fbank[_b][:, 0:4])))
                wdone(idx, last_ev)

            for s2 in range(2):
                slot, rdy, idx = wslice(wvT[:, s2 * 256:(s2 + 1) * 256], 256, CT)
                last_ev = None
                for kvt in range(NKVT):
                    bk = kv_bk()
                    prs = [(xnkv[:, ct * KV + kvt * 128: ct * KV + (kvt + 1) * 128],
                            wbuf[:, slot * 4096 + ct * 256: slot * 4096 + (ct + 1) * 256])
                           for ct in range(CT)]
                    chain(fbank[bk][:, 0:256], bk, prs, [rdy, ev_xn_all])
                    evacuate(bk, "scalar",
                             (lambda s, _b=bk, _k=kvt, _s2=s2:
                              s.copy(vS[:, _k * 512 + _s2 * 256: _k * 512 + (_s2 + 1) * 256],
                                     fbank[_b][:, 0:256])))
                bk = kv_bk()
                prs = [(xns[:, ct * 4: ct * 4 + 4],
                        wbuf[:, slot * 4096 + ct * 256: slot * 4096 + (ct + 1) * 256])
                       for ct in range(CT)]
                last_ev = chain(fbank[bk][0:4, 0:256], bk, prs, [rdy, ev_xn_all])
                evacuate(bk, "scalar",
                         (lambda s, _b=bk, _s2=s2:
                          s.copy(vsS[0:4, _s2 * 256:(_s2 + 1) * 256], fbank[_b][0:4, 0:256])))
                wdone(idx, last_ev)

            cmbT_evts = []
            for tl in range(NQT):
                prs = [(xnkv[:, ct * KV + HALO + tl * 128: ct * KV + HALO + (tl + 1) * 128],
                        gateT[:, ct * 9: ct * 9 + 8]) for ct in range(CT)]
                cev = chain(fbank[3][:, 0:8], 3, prs, [g_ev, ev_xn_all])
                def r1(v):
                    v.reduce_max(pr[:, 8:9], fbank[3][:, 0:8], axis=AX.X, negate=True)
                    v.drain()
                    v.tensor_scalar(pr[:, 0:8], fbank[3][:, 0:8], pr[:, 8:9], None, ALU.add)
                ev_r1 = evacuate(3, "vector", r1)
                def r2(s):
                    s.activation(pr[:, 0:8], pr[:, 0:8], AF.Exp, accum_out=pr[:, 10:11])
                ev_r2 = plan.step("scalar", [ev_r1], r2)
                def r3(v):
                    v.reciprocal(pr[:, 11:12], pr[:, 10:11])
                    v.drain()
                    v.tensor_scalar(pr[:, 0:8], pr[:, 0:8], pr[:, 11:12], None, ALU.mult)
                    v.drain()
                    v.tensor_copy(pr[:, 8:9], pr[:, 0:1])
                    v.memset(pr[:, 9:10], NEG)
                    v.drain()
                    for e8 in range(1, 8):
                        v.tensor_tensor(pr[:, 10:11], pr[:, 8:9], pr[:, e8:e8 + 1], ALU.min)
                        v.drain()
                        v.tensor_tensor(pr[:, 8:9], pr[:, 8:9], pr[:, e8:e8 + 1], ALU.max)
                        v.drain()
                        v.tensor_tensor(pr[:, 9:10], pr[:, 9:10], pr[:, 10:11], ALU.max)
                        v.drain()
                    v.scalar_tensor_tensor(pr[:, 12:20], pr[:, 0:8], pr[:, 9:10], pr[:, 0:8],
                                           ALU.is_ge, ALU.mult)
                ev_r3 = plan.step("vector", [ev_r2], r3)
                tbatch(3, [(fbank[3][0:8, 0:128], pr[:, 12:20])], [ev_r3], idf)
                cmbT_evts.append(
                    evacuate(3, "scalar",
                             (lambda s, _t=tl: s.copy(cmbT[0:8, _t * 128:(_t + 1) * 128],
                                                      fbank[3][0:8, 0:128]))))

            sg_done = []
            for ch in range(2):
                prs = [(gateT[:, ct * 9 + 8: ct * 9 + 9], own_rhs(ct, ch)) for ct in range(CT)]
                chain(fbank[3][0:1, 0:512], 3, prs, [sg_ev, ev_xn_all])
                sg_done.append(
                    evacuate(3, "scalar",
                             (lambda s, _c=ch: s.activation(
                                 cmbrep[32:33, _c * 512:(_c + 1) * 512],
                                 fbank[3][0:1, 0:512], AF.Sigmoid))))

            plan.step("sync", cmbT_evts,
                      lambda g: g.dma_start(out=cmb_scr[0:8, :], in_=cmbT[0:8, :]),
                      kind="dma", sem="sC")
            plan.step("sync", sg_done,
                      lambda g: g.dma_start(out=cmb_scr[8:9, :], in_=sgrow),
                      kind="dma", sem="sC")
            ev_cmb_s = ev_gate_s = plan.now("sC")

            dump("cmbT", cmbT[0:8, :], cmbT_evts)
            dump("kT", kT, [plan.now("sA")])
            dump("vS", vS, [plan.now("sA")])
            dbg_q_ev = plan.now("sD") if dbg else None

            if True:
                def zb_fn(v):
                    v.memset(pb4.bitcast(F32)[:, :], 0.0)
                    v.drain()
                    v.memset(pb5.bitcast(F32)[:, :], 0.0)
                zb = plan.step("vector", banks.wf(4) + banks.wf(5), zb_fn)
                banks.free[4] = zb
                banks.free[5] = zb
            mask_readers = []
            A_par = [cmbrep[:, 0:1024], h1s[:, 0:1024], xio[:, 0:1024]]
            B_par = [bscr, h2c.bitcast(BF16), pbuf[:, 0:2048]]
            S_par = [(stats, 8), (stats, 0), (pr, 0)]
            ln_done_a = plan.now("sA")

            def em_scores(g, hp, j):
                evs = []
                for hh in range(2):
                    h = 2 * hp + hh
                    kvh = h // 4
                    def sc_fn(t, _h=h, _kvh=kvh, _j=j, _hh=hh):
                        q_ap = qao[:, _h * TSC + _j * 128: _h * TSC + (_j + 1) * 128]
                        for b3 in range(3):
                            t.matmul(ps67[:, _hh * 512 + b3 * 128: _hh * 512 + (b3 + 1) * 128],
                                     q_ap,
                                     kT[:, _kvh * KV + (_j + b3) * 128:
                                        _kvh * KV + (_j + b3 + 1) * 128],
                                     start=True, stop=True)
                        return t.matmul(ps67[:, _hh * 512 + 384: _hh * 512 + 388],
                                        q_ap, ksT[:, _kvh * 4: _kvh * 4 + 4],
                                        start=True, stop=True)
                    w = list(banks.wf(6 if hh == 0 else 7)) + [q_ready[hp]]
                    if dbg_q_ev:
                        w.append(dbg_q_ev)
                    cev = plan.step("tensor", w, sc_fn, kind="last")
                    banks.wrote[6 if hh == 0 else 7] = cev
                    evs.append(cev)
                return evs

            def em_m(g, scores_evts, mld_ev):
                par3 = g % 3
                Ap = A_par[par3]
                stt, so = S_par[par3]
                def m_fn(v, _A=Ap, _st=stt, _s=so):
                    srcv = ps67.rearrange("p (h k) -> p h k", h=2)[:, :, 0:388]
                    msk = maskS[:, 0:388].rearrange("p (o k) -> p o k", o=1) \
                        .broadcast_to([128, 2, 388])
                    dst = _A.rearrange("p (h k) -> p h k", h=2)[:, :, 0:388]
                    v.tensor_tensor(dst, srcv, msk, ALU.add)
                    v.drain()
                    v.reduce_max(_st[:, _s:_s + 2], dst, axis=AX.X, negate=True)
                    v.drain()
                    v.tensor_scalar_mul(_st[:, _s + 2:_s + 4], _st[:, _s:_s + 2], SCALE)
                w = scores_evts + [mld_ev, ev_cmb_s]
                if par3 == 2:
                    w.append(ln_done_a)
                ev_m = plan.step("vector", w, m_fn)
                mask_readers.append(ev_m)
                banks.free[6] = ev_m
                banks.free[7] = ev_m
                return ev_m

            def em_e(g, ev_m):
                par3 = g % 3
                Ap = A_par[par3]
                stt, so = S_par[par3]
                def e_fn(s, _A=Ap, _st=stt, _s=so):
                    for hh in range(2):
                        s.activation(_A[:, hh * 512: hh * 512 + 388],
                                     _A[:, hh * 512: hh * 512 + 388], AF.Exp,
                                     bias=_st[:, _s + 2 + hh:_s + 3 + hh],
                                     scale=consts[:, 1:2],
                                     accum_out=_st[:, _s + 4 + hh:_s + 5 + hh])
                        if hh == 0:
                            s.drain()
                return plan.step("scalar", [ev_m], e_fn)

            def em_rn(g, ev_e):
                par3 = g % 3
                Ap = A_par[par3]
                bp = B_par[par3]
                stt, so = S_par[par3]
                def rn_fn(v, _A=Ap, _b=bp, _st=stt, _s=so):
                    v.reciprocal(_st[:, _s + 6:_s + 8], _st[:, _s + 4:_s + 6])
                    v.drain()
                    for hh in range(2):
                        v.tensor_scalar(_b[:, hh * 512: hh * 512 + 388],
                                        _A[:, hh * 512: hh * 512 + 388],
                                        _st[:, _s + 6 + hh:_s + 7 + hh], None, ALU.mult)
                        if hh == 0:
                            v.drain()
                return plan.step("vector", [ev_e], rn_fn)

            def em_tail(g, hp, j, ev_n):
                par2 = g % 2
                par3 = g % 3
                bp = B_par[par3]
                pb, bk = (pb4, 4) if par2 == 0 else (pb5, 5)
                prs = []
                for hh in range(2):
                    for b3 in range(3):
                        prs.append((pb[:, hh * 512 + b3 * 128: hh * 512 + (b3 + 1) * 128],
                                    bp[:, hh * 512 + b3 * 128: hh * 512 + (b3 + 1) * 128]))
                    prs.append((pb[0:4, hh * 512 + 384: (hh + 1) * 512],
                                bp[:, hh * 512 + 384: hh * 512 + 388]))
                tbatch(bk, prs, [ev_n], idb)
                at_ev = evacuate(bk, "scalar",
                                 (lambda s, _pb=pb, _b=bp:
                                  s.copy(_b[:, 1024:2048], _pb[:, 0:1024])))
                def av_fn(t, _hp=hp, _j=j, _b=bp):
                    last = None
                    for hh in range(2):
                        h = 2 * _hp + hh
                        kvh = h // 4
                        o = fbank[3][:, hh * 128:(hh + 1) * 128]
                        for b3 in range(3):
                            last = t.matmul(o,
                                            vS[:, (_j + b3) * 512 + kvh * 128:
                                               (_j + b3) * 512 + (kvh + 1) * 128],
                                            _b[:, 1024 + hh * 512 + b3 * 128:
                                               1024 + hh * 512 + (b3 + 1) * 128],
                                            start=(b3 == 0), stop=False)
                        last = t.matmul(o, vsS[0:4, kvh * 128:(kvh + 1) * 128],
                                        _b[0:4, 1024 + hh * 512 + 384:
                                           1024 + (hh + 1) * 512],
                                        start=False, stop=True)
                    return last
                avw = [at_ev] + banks.wf(3)
                av_ev = plan.step("tensor", avw, av_fn, kind="last")
                banks.wrote[3] = av_ev
                def ao_fn(s, _hp=hp, _j=j):
                    h0 = 2 * _hp
                    s.copy(qao[:, h0 * TSC + _j * 128: h0 * TSC + _j * 128 + 128],
                           fbank[3][:, 0:128])
                    s.drain()
                    s.copy(qao[:, (h0 + 1) * TSC + _j * 128:
                               (h0 + 1) * TSC + _j * 128 + 128],
                           fbank[3][:, 128:256])
                ao_ev = plan.step("scalar", [av_ev], ao_fn)
                banks.free[3] = ao_ev

            q_ready = {}

            def emit_wq_slice(s8):
                slot, rdy, idx = wslice(wqT[:, s8 * 256:(s8 + 1) * 256], 256, CT)
                units = []
                for ftl in range(2):
                    h = 2 * s8 + ftl
                    for ch in range(2):
                        bk = KVROT[(ftl * 2 + ch) % 3]
                        def u(_b=bk, _h=h, _c=ch, _slot=slot, _ftl=ftl, _rdy=rdy):
                            prs = [(wtile(_slot, ct, 256, _ftl), own_rhs(ct, _c))
                                   for ct in range(CT)]
                            ev = chain(fbank[_b], _b, prs, [_rdy, ev_xn_ch[_c]])
                            evacuate(_b, "scalar",
                                     (lambda s, _bb=_b, _hh=_h, _cc=_c:
                                      s.copy(qao[:, _hh * TSC + _cc * 512:
                                                 _hh * TSC + (_cc + 1) * 512],
                                             fbank[_bb])))
                            return ev
                        units.append(u)
                return units, idx

            units0, idx0 = emit_wq_slice(0)
            lastc = None
            for u in units0:
                lastc = u()
            wdone(idx0, lastc)
            q_ready[0] = plan.now("sA")
            pend_units = []
            pend_idx = None

            mld = None
            st_e = {}
            st_rn = {}
            NG = 64
            for g in range(NG):
                hp, j = g // 8, g % 8
                if j == 0 and hp < 7:
                    pend_units, pend_idx = emit_wq_slice(hp + 1)
                    pend_lastc = None
                if j <= 3:
                    mw = list(mask_readers)
                    mld = plan.step("sync", mw,
                                    (lambda gq, _j=j, _sc=sc: gq.dma_start(
                                        out=maskS[:, 0:512], in_=masks[_sc, _j, :, :])),
                                    kind="dma", sem="sM")
                    mask_readers = []
                evs = em_scores(g, hp, j)
                ev_m = em_m(g, evs, mld)
                if pend_units:
                    pend_lastc = pend_units.pop(0)()
                    if not pend_units:
                        wdone(pend_idx, pend_lastc)
                        q_ready[hp + 1] = plan.now("sA")
                if g >= 2:
                    em_tail(g - 2, (g - 2) // 8, (g - 2) % 8, st_rn[g - 2])
                st_e[g] = em_e(g, ev_m)
                if g >= 1:
                    st_rn[g - 1] = em_rn(g - 1, st_e[g - 1])
            st_rn[NG - 1] = em_rn(NG - 1, st_e[NG - 1])
            em_tail(NG - 2, 7, 6, st_rn[NG - 2])
            em_tail(NG - 1, 7, 7, st_rn[NG - 1])

            ao_all = plan.now("sA")
            ao_allA = plan.now("sV")
            attn_done_v = plan.cnt["sV"]
            attn_done_a = plan.cnt["sA"]
            av_guard = [plan.cnt["sV"]]
            dump("ao", qao, [ao_all])
            dbg_ao_ev = plan.now("sD") if dbg else None

            out_guard = plan.now("sO")
            wo_last_chain = None
            for s8 in range(8):
                slot, rdy, idx = wslice(woT[:, s8 * 256:(s8 + 1) * 256], 256, CT)
                last_ev = None
                for mt in range(2):
                    ctl = 2 * s8 + mt
                    for ch in range(2):
                        bk = (s8 * 4 + mt * 2 + ch) % 6
                        prs = [(wtile(slot, ct, 256, mt),
                                qao[:, ct * TSC + ch * 512: ct * TSC + (ch + 1) * 512])
                               for ct in range(CT)]
                        w = [rdy, ao_all, ao_allA] + ([dbg_ao_ev] if dbg_ao_ev else [])
                        last_ev = chain(fbank[bk], bk, prs, w)
                        evacuate(bk, "vector",
                                 (lambda v, _b=bk, _c=ctl, _ch=ch:
                                  v.tensor_copy(
                                      acc[:, _c * TSC + _ch * 512: _c * TSC + (_ch + 1) * 512],
                                      fbank[_b])),
                                 extra_waits=[out_guard])
                wo_last_chain = last_ev
                wdone(idx, last_ev)

            dump("acc_wo", acc, [plan.now("sV")])
            dbg_accwo_ev = plan.now("sD") if dbg else None

            state = {"prev_h2c": None, "dbg_h": dbg_accwo_ev}
            state["wo_guard"] = wo_last_chain
            h_slot_ev = [None, None]

            def h_ap2(pe, ft):
                if pe % 2 == 0:
                    return hT[:, ft * TSC: (ft + 1) * TSC]
                if ft < 5:
                    return kT[:, ft * TSC: (ft + 1) * TSC]
                if ft < 10:
                    return vS[:, (ft - 5) * TSC: (ft - 4) * TSC]
                return qao[:, NFT * TSC: NFT * TSC + TSC]

            def h_ap(pe, ft, ch):
                if pe % 2 == 0:
                    return hT[:, ft * TSC + ch * 512: ft * TSC + (ch + 1) * 512]
                if ft < 5:
                    return kT[:, ft * TSC + ch * 512: ft * TSC + (ch + 1) * 512]
                if ft < 10:
                    return vS[:, (ft - 5) * TSC + ch * 512: (ft - 5) * TSC + (ch + 1) * 512]
                return qao[:, NFT * TSC + ch * 512: NFT * TSC + (ch + 1) * 512]

            def em_rep(pe):
                shared = pe >= E
                rep_src = cmb_scr[(8 if shared else pe):(9 if shared else pe + 1), :] \
                    .partition_broadcast(128)
                rw = [ev_cmb_s, ao_all, ("sV", av_guard[0])]
                if state["prev_h2c"] is not None:
                    rw.append(state["prev_h2c"])
                return plan.step("sync", rw,
                                 (lambda g, _s=rep_src: g.dma_start(out=cmbrep, in_=_s)),
                                 kind="dma", sem="sR")

            def em_fc_slice(pe, si, fbase, wdt, hp_i0, ev_rep, h_slot_ev, h_evts):
                shared = pe >= E
                k4 = pe - E
                if shared:
                    s1h = sf1T[:, k4 * FMOE + fbase: k4 * FMOE + fbase + wdt]
                    s2h = sf2T[:, k4 * FMOE + fbase: k4 * FMOE + fbase + wdt]
                else:
                    s1h = fc1T[pe, :, fbase: fbase + wdt]
                    s2h = fc2T[pe, :, fbase: fbase + wdt]
                slot1, rdy1, idx1 = wslice(s1h, wdt, CT)
                slot2, rdy2, idx2 = wslice(s2h, wdt, CT)
                last1 = last2 = None
                hp_i = hp_i0
                for ftl in range(wdt // 128):
                    fglob = (fbase // 128) + ftl
                    evs12 = []
                    for ch in range(2):
                        par = hp_i % 3
                        b1, b2 = [(0, 1), (2, 3), (4, 5)][par]
                        sl = ch
                        prs1 = [(wtile(slot1, ct, wdt, ftl), own_rhs(ct, ch))
                                for ct in range(CT)]
                        last1 = chain(fbank[b1], b1, prs1, [rdy1, ev_xn_ch[ch]])
                        slot_guard = [h_slot_ev[0]] if h_slot_ev[0] else []
                        ev_h1 = evacuate(b1, "scalar",
                                         (lambda s, _b=b1, _sl=sl:
                                          s.activation(h1s[:, _sl * 512:(_sl + 1) * 512],
                                                       fbank[_b], AF.Silu)),
                                         extra_waits=slot_guard)
                        prs2 = [(wtile(slot2, ct, wdt, ftl), own_rhs(ct, ch))
                                for ct in range(CT)]
                        last2 = chain(fbank[b2], b2, prs2, [rdy2, ev_xn_ch[ch]])
                        ev_h2 = evacuate(b2, "vector",
                                         (lambda v, _b=b2, _sl=sl, _ch=ch:
                                          v.tensor_tensor(h2c[:, _sl * 512:(_sl + 1) * 512],
                                                          fbank[_b],
                                                          cmbrep[:, _ch * 512:(_ch + 1) * 512],
                                                          ALU.mult)),
                                         extra_waits=[ev_rep] + slot_guard)
                        state["prev_h2c"] = ev_h2
                        evs12 += [ev_h1, ev_h2]
                        hp_i += 1
                    hw = evs12
                    if state.get("wo_guard") is not None:
                        hw.append(state["wo_guard"])
                    if state["dbg_h"]:
                        hw.append(state["dbg_h"])
                    hstep = plan.step(
                        "vector", hw,
                        (lambda v, _pe=pe, _f=fglob:
                         v.tensor_tensor(
                             h_ap2(_pe, _f),
                             h1s[:, 0:1024],
                             h2c.bitcast(BF16)[:, 0:1024] if False else h2c[:, 0:1024],
                             ALU.mult)))
                    h_evts.append(hstep)
                    h_slot_ev[0] = hstep
                wdone(idx1, last1)
                wdone(idx2, last2)
                return hp_i

            pq = {"i": 0, "done": {}}

            def em_proj_slice(pe, s16, ev_h_all):
                shared = pe >= E
                k4 = pe - E
                if shared:
                    ph = sprT[k4 * FMOE:(k4 + 1) * FMOE, s16 * 128:(s16 + 1) * 128]
                else:
                    ph = prjT[pe, :, s16 * 128:(s16 + 1) * 128]
                i = pq["i"]
                slot = i % 2
                pw = [pq["done"][i - 2]] if i >= 2 else []
                pw.append(ao_all)
                src3 = ph.rearrange("(t p) f -> p t f", p=128)
                dst3 = pbuf[:, slot * 1408:(slot + 1) * 1408] \
                    .rearrange("p (t f) -> p t f", t=NFT)
                rdy = plan.step("gpsimd", pw,
                                (lambda g, _d=dst3, _s=src3: g.dma_start(out=_d, in_=_s)),
                                kind="dma", sem=("sP0" if slot == 0 else "sP1"))
                pq["i"] = i + 1
                last_ev = None
                for ch in range(2):
                    bk = 6 if ch == 0 else 7
                    prs = [(pbuf[:, slot * 1408 + ft * 128:(slot * 1408 + (ft + 1) * 128)],
                            h_ap(pe, ft, ch))
                           for ft in range(NFT)]
                    last_ev = chain(fbank[bk], bk, prs, [rdy, ev_h_all])
                    evacuate(bk, "vector",
                             (lambda v, _b=bk, _c=s16, _ch=ch:
                              v.tensor_tensor(
                                  acc[:, _c * TSC + _ch * 512: _c * TSC + (_ch + 1) * 512],
                                  fbank[_b],
                                  acc[:, _c * TSC + _ch * 512: _c * TSC + (_ch + 1) * 512],
                                  ALU.add)))
                pq["done"][i] = last_ev
                return last_ev

            FC_BASES = []
            fb_ = 0
            for w_ in FC_SLICES:
                FC_BASES.append(fb_)
                fb_ += w_
            prev_hall = None
            state["dbg_h"] = None
            for pe in range(E + 4 + 1):
                do_fc = pe < E + 4
                if do_fc:
                    if pe < E or pe == E:
                        ev_rep = em_rep(pe)
                    h_evts = []
                    hp_i = 0
                nfc = 6 if do_fc else 0
                npj = 16 if pe > 0 else 0
                fc_i, pj_i = 0, 0
                seq = []
                while fc_i < nfc or pj_i < npj:
                    if fc_i < nfc:
                        seq.append(("fc", fc_i)); fc_i += 1
                    want = npj if fc_i >= nfc else (fc_i * npj + nfc - 1) // max(nfc, 1)
                    while pj_i < min(want, npj):
                        seq.append(("pj", pj_i)); pj_i += 1
                for kind, i in seq:
                    if kind == "fc":
                        hp_i = em_fc_slice(pe, i, FC_BASES[i], FC_SLICES[i],
                                           hp_i, ev_rep, h_slot_ev, h_evts)
                    else:
                        em_proj_slice(pe - 1, i, prev_hall)
                if do_fc:
                    prev_hall = ("sV", max(v for (_, v) in h_evts))
                    if pe == 0:
                        dump("h0", hT, [prev_hall])
                        state["dbg_h"] = plan.now("sD") if dbg else None

            dump("acc_fin", acc, [plan.now("sV")])

            acc_ready = plan.now("sV")
            prev_xa = None
            for tl in range(NQT):
                tok = tl * 128
                dma_waits = []
                for q4 in range(4):
                    bk = q4 % 4
                    pbk = fbank[bk]
                    prs = [(pbk[:, i * 128:(i + 1) * 128],
                            acc[:, (q4 * 4 + i) * TSC + tok: (q4 * 4 + i) * TSC + tok + 128])
                           for i in range(4)]
                    tbatch(bk, prs, [acc_ready], idf)
                    dma_waits.append(
                        evacuate(bk, "scalar",
                                 (lambda s, _q=q4, _b=bk, _tok=tok:
                                  s.copy(acc3[:, _q * 4:(_q + 1) * 4, _tok:_tok + 128],
                                         fbank[_b].rearrange("p (i r) -> p i r", i=4)))))
                xsrc = x_own[g0 + tok: g0 + tok + 128, :].rearrange("p (t r) -> p t r", t=CT)
                xw = list(dma_waits)
                if prev_xa is not None:
                    xw.append(prev_xa)
                xaev = plan.step("gpsimd", xw,
                                 (lambda g, _s=xsrc, _tok=tok: g.dma_start(
                                     out=acc3[:, :, _tok:_tok + 128], in_=_s,
                                     accum_op=ALU.add)),
                                 kind="dma", sem="sXA")
                prev_xa = xaev
                oev = plan.step("sync", [xaev],
                                (lambda g, _t=tl, _g0=g0, _tok=tok: g.dma_start(
                                    out=out[_g0 + _t * 128: _g0 + (_t + 1) * 128, :]
                                    .rearrange("p (t r) -> p t r", t=CT),
                                    in_=acc3[:, :, _tok:_tok + 128])),
                                kind="dma", sem="sO")
                last_out_ev = oev

        plan.step("sync", [last_out_ev], lambda g: g.nop(), kind="none")

        block.sync(lambda e: plan.run("sync", e, sems))
        block.gpsimd(lambda e: plan.run("gpsimd", e, sems))
        block.vector(lambda e: plan.run("vector", e, sems))
        block.scalar(lambda e: plan.run("scalar", e, sems))
        block.tensor(lambda e: plan.run("tensor", e, sems))

    return nc, dbg_outs


def make_masks(half):
    m = np.full((NSC, 4, 128, 512), NEG, dtype=np.float32)
    ql = np.arange(128)
    for sc in range(NSC):
        for j in range(4):
            gq = half * 16 + sc * 8 + j
            q = 128 * gq + ql[:, None]
            for b3 in range(3):
                kt = gq + b3 - 2
                if kt < 0:
                    continue
                k = 128 * kt + np.arange(128)[None, :]
                valid = (k <= q) & ((k >= q - 255) | (k < 4))
                blk = m[sc, j, :, b3 * 128:(b3 + 1) * 128]
                blk[valid] = 0.0
            if gq >= 3:
                k = np.arange(4)[None, :]
                valid = np.broadcast_to(k <= q, (128, 4))
                blk = m[sc, j, :, 384:388]
                blk[valid] = 0.0
    return m


_NC_CACHE = {}


def kernel(L0_attn_x, L0_attn_wq, L0_attn_wk, L0_attn_wv, L0_attn_wo,
           L0_ffn_gate_w, L0_ffn_expert_fc1, L0_ffn_expert_fc2, L0_ffn_expert_proj,
           L0_ffn_shared_fc1, L0_ffn_shared_fc2, L0_ffn_shared_proj,
           L0_ffn_shared_gate_w):
    x = np.asarray(L0_attn_x, dtype=np.float32)
    tp = lambda w: np.ascontiguousarray(np.asarray(w, dtype=np.float32).T)
    wqT_, wkT_, wvT_, woT_ = tp(L0_attn_wq), tp(L0_attn_wk), tp(L0_attn_wv), tp(L0_attn_wo)
    gT = np.ascontiguousarray(np.concatenate(
        [np.asarray(L0_ffn_gate_w, dtype=np.float32).T,
         np.asarray(L0_ffn_shared_gate_w, dtype=np.float32).T], axis=1))
    f1T = np.ascontiguousarray(np.asarray(L0_ffn_expert_fc1, dtype=np.float32).transpose(0, 2, 1))
    f2T = np.ascontiguousarray(np.asarray(L0_ffn_expert_fc2, dtype=np.float32).transpose(0, 2, 1))
    pjT = np.ascontiguousarray(np.asarray(L0_ffn_expert_proj, dtype=np.float32).transpose(0, 2, 1))
    s1T, s2T, spT = tp(L0_ffn_shared_fc1), tp(L0_ffn_shared_fc2), tp(L0_ffn_shared_proj)

    if "nc" not in _NC_CACHE:
        _NC_CACHE["nc"] = build(dbg=0)[0]
    nc = _NC_CACHE["nc"]

    zeros_halo = np.zeros((HALO, D), dtype=np.float32)
    masks_by_half = [make_masks(0), make_masks(1)]
    in_maps = []
    for c in range(8):
        b, half = c // 2, c % 2
        in_maps.append({
            "x_own": np.ascontiguousarray(x[b, half * 2048:(half + 1) * 2048]),
            "x_halo": zeros_halo if half == 0 else np.ascontiguousarray(x[b, 1792:2048]),
            "x_sink": np.ascontiguousarray(x[b, 0:4]),
            "wqT": wqT_, "wkT": wkT_, "wvT": wvT_, "woT": woT_,
            "gateT": gT,
            "efc1T": f1T, "efc2T": f2T, "eprjT": pjT,
            "sfc1T": s1T, "sfc2T": s2T, "sprjT": spT,
            "masks": masks_by_half[half],
        })
    res = run_bass_kernel_spmd(nc, in_maps, core_ids=list(range(8)))
    outp = np.empty((4, 4096, D), dtype=np.float32)
    for c in range(8):
        b, half = c // 2, c % 2
        outp[b, half * 2048:(half + 1) * 2048] = res.results[c]["out"]
    return outp


# version 19
# speedup vs baseline: 1.1390x; 1.1390x over previous
import numpy as np
import ml_dtypes
import concourse.bass as bass
import concourse.mybir as mybir
from concourse.bass_utils import run_bass_kernel_spmd
from contextlib import ExitStack

F32 = mybir.dt.float32
BF16 = mybir.dt.bfloat16
AF = mybir.ActivationFunctionType
ALU = mybir.AluOpType
AX = mybir.AxisListType

D = 2048
CT = 16
TSC = 1024
NSC = 2
HALO = 256
KV = HALO + TSC
NKVT = KV // 128
NQT = TSC // 128
NFT = 11
E, FMOE, FSH = 8, 1408, 5632
SCALE = 1.0 / float(np.sqrt(128.0))
NEG = -1.0e30
EPS = 1.0e-5
FC_SLICES = [256, 256, 256, 256, 256, 128]
ENGS = ("sync", "gpsimd", "vector", "scalar", "tensor")
ESEM = {"sync": "sS", "gpsimd": "sG", "vector": "sV", "scalar": "sA", "tensor": "sT"}


SEMS = ("sS", "sG", "sV", "sA", "sT", "sW0", "sW1", "sP0", "sP1", "sGT",
        "sX", "sXA", "sM", "sR", "sC", "sI", "sO", "sD")


class Plan:
    def __init__(self):
        self.steps = {e: [] for e in ENGS}
        self.cnt = {s: 0 for s in SEMS}

    def step(self, eng, waits, fn, kind="drain", sem=None):
        if sem is None:
            sem = ESEM[eng]
        if kind == "dma":
            self.cnt[sem] += 16
        elif kind in ("last", "drain"):
            self.cnt[sem] += 1
        ev = (sem, self.cnt[sem]) if kind != "none" else None
        ws = tuple((s, v) for (s, v) in waits if v > 0)
        self.steps[eng].append((ws, fn, kind, sem, self.cnt[sem]))
        return ev

    def now(self, sem):
        return (sem, self.cnt[sem])

    def run(self, name, e, sems):
        waited = {}
        for ws, fn, kind, sem, _ in self.steps[name]:
            for (s, v) in ws:
                if v > waited.get(s, 0):
                    e.wait_ge(sems[s], v)
                    waited[s] = v
            ins = fn(e)
            if kind == "dma":
                ins.then_inc(sems[sem], 16)
            elif kind == "last":
                ins.then_inc(sems[sem], 1)
            elif kind == "drain":
                e.drain().then_inc(sems[sem], 1)


class Banks:
    def __init__(self):
        self.free = {}
        self.wrote = {}

    def wf(self, b):
        ev = self.free.get(b)
        return [ev] if ev else []

    def ww(self, b):
        return [self.wrote[b]]


def build(dbg=0, n_sc=NSC):
    nc = bass.Bass("TRN2", target_bir_lowering=False)
    ap = lambda t: t.ap()

    x_own = ap(nc.dram_tensor("x_own", [2048, D], F32, kind="ExternalInput"))
    x_halo = ap(nc.dram_tensor("x_halo", [HALO, D], F32, kind="ExternalInput"))
    x_sink = ap(nc.dram_tensor("x_sink", [4, D], F32, kind="ExternalInput"))
    wqT = ap(nc.dram_tensor("wqT", [D, 2048], F32, kind="ExternalInput"))
    wkT = ap(nc.dram_tensor("wkT", [D, 512], F32, kind="ExternalInput"))
    wvT = ap(nc.dram_tensor("wvT", [D, 512], F32, kind="ExternalInput"))
    woT = ap(nc.dram_tensor("woT", [2048, D], F32, kind="ExternalInput"))
    gTh = ap(nc.dram_tensor("gateT", [D, 9], F32, kind="ExternalInput"))
    fc1T = ap(nc.dram_tensor("efc1T", [E, D, FMOE], F32, kind="ExternalInput"))
    fc2T = ap(nc.dram_tensor("efc2T", [E, D, FMOE], F32, kind="ExternalInput"))
    prjT = ap(nc.dram_tensor("eprjT", [E, FMOE, D], F32, kind="ExternalInput"))
    sf1T = ap(nc.dram_tensor("sfc1T", [D, FSH], F32, kind="ExternalInput"))
    sf2T = ap(nc.dram_tensor("sfc2T", [D, FSH], F32, kind="ExternalInput"))
    sprT = ap(nc.dram_tensor("sprjT", [FSH, D], F32, kind="ExternalInput"))
    masks = ap(nc.dram_tensor("masks", [NSC, 4, 128, 512], F32, kind="ExternalInput"))
    out = ap(nc.dram_tensor("out", [2048, D], F32, kind="ExternalOutput"))

    cmb_scr = ap(nc.dram_tensor("cmb_scr", [E + 1, TSC], F32, kind="Internal"))

    idf_h = ap(nc.inline_tensor(np.eye(128, dtype=np.float32), name="idf_h"))
    idb_h = ap(nc.inline_tensor(np.eye(128, dtype=ml_dtypes.bfloat16), name="idb_h"))

    dbg_outs = {}

    with ExitStack() as st:
        en = st.enter_context
        S = lambda name, shape, dt: en(nc.sbuf_tensor(name, shape, dt))[:]
        xnkv = S("xnkv", [128, CT * KV], BF16)
        acc = S("acc", [128, CT * TSC], F32)
        qao = S("qao", [128, CT * TSC], BF16)
        hT = qao[:, 0:NFT * TSC]
        kT = S("kT", [128, 4 * KV], BF16)
        vS = S("vS", [128, NKVT * 512], BF16)
        wbuf = S("wbuf", [128, 2 * 4096], BF16)
        xio = S("xio", [128, 2048], F32)
        bscr = S("bscr", [128, 2048], BF16)
        maskS = S("maskS", [128, 512], F32)
        cmbrep = S("cmbrep", [128, TSC], F32)
        A = cmbrep
        pbuf = S("pbuf", [128, 2 * 1408], BF16)
        h1s = S("h1s", [128, 2 * 512], F32)
        h2c = S("h2c", [128, 2 * 512], F32)
        xns = S("xns", [128, CT * 4], BF16)
        ksT = S("ksT", [128, 4 * 4], BF16)
        vsS = S("vsS", [4, 512], BF16)
        gateT = S("gateTs", [128, CT * 9], BF16)
        stats = S("stats", [128, 16], F32)
        pr = S("pr", [128, 24], F32)
        consts = S("consts", [128, 2], F32)
        idf = S("idf", [128, 128], F32)
        idb = S("idb", [128, 128], BF16)

        cmbT = cmbrep[0:8, :]
        sgrow = cmbrep[32:33, 0:1024]
        xnkv3 = xnkv.rearrange("p (t r) -> p t r", t=CT)
        xns3 = xns.rearrange("p (t r) -> p t r", t=CT)
        acc3 = acc.rearrange("p (t r) -> p t r", t=CT)

        import os as _os
        if _os.environ.get("KPRINT"):
            print("sbuf_bytes_remaining:", nc.sbuf_bytes_remaining)
        assert nc.sbuf_bytes_remaining >= 0, f"SBUF over by {-nc.sbuf_bytes_remaining}"

        P = lambda name, shape, dt: en(nc.psum_tensor(name, shape, dt))[:]
        ps01 = P("ps01", [128, 1024], F32)
        ps23 = P("ps23", [128, 1024], F32)
        pb4 = P("pb4", [128, 1024], BF16)
        pb5 = P("pb5", [128, 1024], BF16)
        ps67 = P("ps67", [128, 1024], F32)
        fbank = {0: ps01[:, 0:512], 1: ps01[:, 512:1024], 2: ps23[:, 0:512],
                 3: ps23[:, 512:1024], 6: ps67[:, 0:512], 7: ps67[:, 512:1024],
                 4: pb4.bitcast(F32), 5: pb5.bitcast(F32)}

        sems = {}
        for sn in SEMS:
            sems[sn] = en(nc.semaphore(sn))
        block = en(nc.Block())

        plan = Plan()
        banks = Banks()
        dbg_guard = {"ev": None}

        def init_consts(v):
            v.memset(consts[:, 0:1], EPS)
            v.drain()
            v.memset(consts[:, 1:2], SCALE)
            v.drain()
            v.memset(pb4.bitcast(F32)[:, :], 0.0)
            v.drain()
            v.memset(pb5.bitcast(F32)[:, :], 0.0)
        ev_consts = plan.step("vector", [], init_consts)
        banks.free[4] = ev_consts
        banks.free[5] = ev_consts
        ev_idf = plan.step("sync", [], lambda g: g.dma_start(out=idf, in_=idf_h),
                           kind="dma", sem="sI")
        ev_idb = plan.step("sync", [], lambda g: g.dma_start(out=idb, in_=idb_h),
                           kind="dma", sem="sI")
        wq_state = {"i": 0, "done": {}}

        def wslice(src2d, width, nt):
            i = wq_state["i"]
            slot = i % 2
            waits = [wq_state["done"][i - 2]] if i >= 2 else []
            src3 = src2d.rearrange("(t p) f -> p t f", p=128)
            dst3 = wbuf[:, slot * 4096: slot * 4096 + nt * width] \
                .rearrange("p (t f) -> p t f", t=nt)
            ev = plan.step("gpsimd", waits,
                           (lambda g, _d=dst3, _s=src3: g.dma_start(out=_d, in_=_s)),
                           kind="dma", sem=("sW0" if slot == 0 else "sW1"))
            wq_state["i"] = i + 1
            return slot, ev, i

        def wdone(idx, ev):
            wq_state["done"][idx] = ev

        def wtile(slot, ct, w, ftl):
            o = slot * 4096 + ct * w + ftl * 128
            return wbuf[:, o: o + 128]

        def chain(bank_ap, bank, pairs, extra_waits):
            waits = list(extra_waits) + banks.wf(bank)
            def fn(t, _pairs=tuple(pairs), _out=bank_ap):
                last = None
                n = len(_pairs)
                for i, (l, r) in enumerate(_pairs):
                    last = t.matmul(_out, l, r, start=(i == 0), stop=(i == n - 1))
                return last
            ev = plan.step("tensor", waits, fn, kind="last")
            banks.wrote[bank] = ev
            return ev

        def evacuate(bank, eng, fn, extra_waits=()):
            waits = list(extra_waits) + banks.ww(bank)
            if dbg_guard["ev"]:
                waits.append(dbg_guard["ev"])
            ev = plan.step(eng, waits, fn, kind="drain")
            banks.free[bank] = ev
            return ev

        def tbatch(bank, pairs, extra_waits, ident):
            waits = list(extra_waits) + banks.wf(bank) + [ev_idb, ev_idf]
            def fn(t, _pairs=tuple(pairs), _id=ident):
                last = None
                for (o, i) in _pairs:
                    last = t.transpose(o, i, _id)
                return last
            ev = plan.step("tensor", waits, fn, kind="last")
            banks.wrote[bank] = ev
            return ev

        def dump(name, src_ap, waits):
            if not dbg:
                return None
            if name in dbg_outs:
                d = dbg_outs[name]
            else:
                d = ap(nc.dram_tensor("dbg_" + name, list(src_ap.shape), src_ap.dtype,
                                      kind="ExternalOutput"))
                dbg_outs[name] = d
            return plan.step("sync", waits,
                             (lambda g, _s=src_ap, _d=d: g.dma_start(out=_d, in_=_s)),
                             kind="dma", sem="sD")

        last_out_ev = None
        xio_readers = []
        prev_bscr_readers = []

        ln_results = {}
        attn_done = [0, 0]

        def emit_ln(sc_ln):
            nonlocal xio_readers, prev_bscr_readers
            g0_ln = sc_ln * TSC
            xn_evts = []
            xn_chunk_evts = [[], []]
            for seq_t in range(11):
                tt = seq_t + 2 if seq_t < 8 else (seq_t - 8 if seq_t < 10 else 10)
                if tt < 2:
                    src = x_halo[tt * 128:(tt + 1) * 128, :] if sc_ln == 0 else \
                          x_own[TSC - HALO + tt * 128: TSC - HALO + (tt + 1) * 128, :]
                    Pn, dstc = 128, tt * 128
                elif tt < 10:
                    src = x_own[g0_ln + (tt - 2) * 128: g0_ln + (tt - 1) * 128, :]
                    Pn, dstc = 128, HALO + (tt - 2) * 128
                else:
                    src, Pn, dstc = x_sink[0:4, :], 4, 0

                lw = list(xio_readers)
                if sc_ln > 0:
                    lw += [("sV", attn_done[0]), ("sA", attn_done[1])]
                ld = plan.step("sync", lw,
                               (lambda g, _s=src, _p=Pn: g.dma_start(out=xio[0:_p, :], in_=_s)),
                               kind="dma", sem="sX")
                xio_readers = []

                def v1(v, p=Pn):
                    v.reduce_sum(stats[0:p, 0:1], xio[0:p, :], axis=AX.X)
                    v.drain()
                    v.tensor_scalar_mul(stats[0:p, 1:2], stats[0:p, 0:1], 1.0 / D)
                    v.drain()
                    v.tensor_mul(stats[0:p, 2:3], stats[0:p, 1:2], stats[0:p, 1:2])
                ev_v1 = plan.step("vector", [ld], v1)

                def s1(s, p=Pn):
                    s.activation(bscr[0:p, :], xio[0:p, :], AF.Square,
                                 accum_out=stats[0:p, 3:4])
                ev_s1 = plan.step("scalar", [ld, ev_consts] + prev_bscr_readers, s1)

                def v2(v, p=Pn):
                    v.tensor_scalar(stats[0:p, 4:5], stats[0:p, 3:4], 1.0 / D,
                                    stats[0:p, 2:3], ALU.mult, ALU.subtract)
                ev_v2 = plan.step("vector", [ev_s1, ev_v1], v2)

                def s2(s, p=Pn):
                    s.activation(stats[0:p, 5:6], stats[0:p, 4:5], AF.Sqrt,
                                 bias=consts[0:p, 0:1])
                ev_s2 = plan.step("scalar", [ev_v2], s2)

                def v3(v, p=Pn):
                    v.reciprocal(stats[0:p, 6:7], stats[0:p, 5:6])
                    v.drain()
                    v.tensor_scalar(bscr[0:p, :], xio[0:p, :], stats[0:p, 1:2],
                                    stats[0:p, 6:7], ALU.subtract, ALU.mult)
                ev_v3 = plan.step("vector", [ev_s2] + prev_bscr_readers, v3)
                xio_readers += [ev_v3, ev_s1]
                prev_bscr_readers = []

                ident = idb if Pn == 128 else idb[0:4, 0:4]
                for half in range(2):
                    pb, bk = (pb4, 4) if half == 0 else (pb5, 5)
                    prs = [(pb[:, i * 128: i * 128 + Pn],
                            bscr[0:Pn, (half * 8 + i) * 128: (half * 8 + i + 1) * 128])
                           for i in range(8)]
                    tev = tbatch(bk, prs, [ev_v3], ident)
                    prev_bscr_readers.append(tev)
                    if tt < 10:
                        def ev_fn(v, h=half, _pb=pb, c=dstc):
                            d = xnkv3[:, h * 8:(h + 1) * 8, c:c + 128]
                            s_ = _pb.rearrange("q (i r) -> q i r", i=8)
                            v.tensor_copy(d, s_)
                    else:
                        def ev_fn(v, h=half, _pb=pb):
                            d = xns3[:, h * 8:(h + 1) * 8, :]
                            s_ = _pb.rearrange("q (i r) -> q i r", i=8)[:, :, 0:4]
                            v.tensor_copy(d, s_)
                    xev = evacuate(bk, "vector", ev_fn)
                    xn_evts.append(xev)
                    if 2 <= tt < 10:
                        xn_chunk_evts[(tt - 2) // 4].append(xev)


            ev_xn_all = ("sV", max(v for (_, v) in xn_evts))
            ev_xn_ch = [("sV", max(v for (_, v) in xn_chunk_evts[c])) for c in range(2)]
            ln_results[sc_ln] = (ev_xn_ch, ev_xn_all)


        emit_ln(0)

        for sc in range(n_sc):
            g0 = sc * TSC
            if dbg and sc > 0:
                dbg_guard["ev"] = plan.now("sD")
            if sc not in ln_results:
                emit_ln(sc)
            ev_xn_ch, ev_xn_all = ln_results[sc]

            if sc == 0:
                g_ev = plan.step("gpsimd", [], lambda g: g.dma_start(
                    out=gateT.rearrange("p (t f) -> p t f", t=CT),
                    in_=gTh.rearrange("(t p) f -> p t f", p=128)), kind="dma", sem="sGT")
                sg_ev = g_ev

            def own_rhs(ct, ch):
                o = ct * KV + HALO + ch * 512
                return xnkv[:, o: o + 512]

            KVROT = [0, 1, 2]
            kvi = [0]

            def kv_bk():
                b = KVROT[kvi[0] % 3]
                kvi[0] += 1
                return b

            kchunks = [(0, 256), (256, 512), (768, 512)]
            for s2 in range(2):
                slot, rdy, idx = wslice(wkT[:, s2 * 256:(s2 + 1) * 256], 256, CT)
                last_ev = None
                for ftl in range(2):
                    kvh = 2 * s2 + ftl
                    for ci, (off, ln) in enumerate(kchunks):
                        bk = kv_bk()
                        prs = [(wtile(slot, ct, 256, ftl), xnkv[:, ct * KV + off: ct * KV + off + ln])
                               for ct in range(CT)]
                        chain(fbank[bk][:, 0:ln], bk, prs, [rdy, ev_xn_all])
                        evacuate(bk, "scalar",
                                 (lambda s, _b=bk, _k=kvh, _o=off, _l=ln:
                                  s.copy(kT[:, _k * KV + _o: _k * KV + _o + _l],
                                         fbank[_b][:, 0:_l])))
                    bk = kv_bk()
                    prs = [(wtile(slot, ct, 256, ftl), xns[:, ct * 4: ct * 4 + 4])
                           for ct in range(CT)]
                    last_ev = chain(fbank[bk][:, 0:4], bk, prs, [rdy, ev_xn_all])
                    evacuate(bk, "scalar",
                             (lambda s, _b=bk, _k=kvh:
                              s.copy(ksT[:, _k * 4: _k * 4 + 4], fbank[_b][:, 0:4])))
                wdone(idx, last_ev)

            for s2 in range(2):
                slot, rdy, idx = wslice(wvT[:, s2 * 256:(s2 + 1) * 256], 256, CT)
                last_ev = None
                for kvt in range(NKVT):
                    bk = kv_bk()
                    prs = [(xnkv[:, ct * KV + kvt * 128: ct * KV + (kvt + 1) * 128],
                            wbuf[:, slot * 4096 + ct * 256: slot * 4096 + (ct + 1) * 256])
                           for ct in range(CT)]
                    chain(fbank[bk][:, 0:256], bk, prs, [rdy, ev_xn_all])
                    evacuate(bk, "scalar",
                             (lambda s, _b=bk, _k=kvt, _s2=s2:
                              s.copy(vS[:, _k * 512 + _s2 * 256: _k * 512 + (_s2 + 1) * 256],
                                     fbank[_b][:, 0:256])))
                bk = kv_bk()
                prs = [(xns[:, ct * 4: ct * 4 + 4],
                        wbuf[:, slot * 4096 + ct * 256: slot * 4096 + (ct + 1) * 256])
                       for ct in range(CT)]
                last_ev = chain(fbank[bk][0:4, 0:256], bk, prs, [rdy, ev_xn_all])
                evacuate(bk, "scalar",
                         (lambda s, _b=bk, _s2=s2:
                          s.copy(vsS[0:4, _s2 * 256:(_s2 + 1) * 256], fbank[_b][0:4, 0:256])))
                wdone(idx, last_ev)

            cmbT_evts = []
            for tl in range(NQT):
                prs = [(xnkv[:, ct * KV + HALO + tl * 128: ct * KV + HALO + (tl + 1) * 128],
                        gateT[:, ct * 9: ct * 9 + 8]) for ct in range(CT)]
                cev = chain(fbank[3][:, 0:8], 3, prs, [g_ev, ev_xn_all])
                def r1(v):
                    v.reduce_max(pr[:, 8:9], fbank[3][:, 0:8], axis=AX.X, negate=True)
                    v.drain()
                    v.tensor_scalar(pr[:, 0:8], fbank[3][:, 0:8], pr[:, 8:9], None, ALU.add)
                ev_r1 = evacuate(3, "vector", r1)
                def r2(s):
                    s.activation(pr[:, 0:8], pr[:, 0:8], AF.Exp, accum_out=pr[:, 10:11])
                ev_r2 = plan.step("scalar", [ev_r1], r2)
                def r3(v):
                    v.reciprocal(pr[:, 11:12], pr[:, 10:11])
                    v.drain()
                    v.tensor_scalar(pr[:, 0:8], pr[:, 0:8], pr[:, 11:12], None, ALU.mult)
                    v.drain()
                    v.tensor_copy(pr[:, 8:9], pr[:, 0:1])
                    v.memset(pr[:, 9:10], NEG)
                    v.drain()
                    for e8 in range(1, 8):
                        v.tensor_tensor(pr[:, 10:11], pr[:, 8:9], pr[:, e8:e8 + 1], ALU.min)
                        v.drain()
                        v.tensor_tensor(pr[:, 8:9], pr[:, 8:9], pr[:, e8:e8 + 1], ALU.max)
                        v.drain()
                        v.tensor_tensor(pr[:, 9:10], pr[:, 9:10], pr[:, 10:11], ALU.max)
                        v.drain()
                    v.scalar_tensor_tensor(pr[:, 12:20], pr[:, 0:8], pr[:, 9:10], pr[:, 0:8],
                                           ALU.is_ge, ALU.mult)
                ev_r3 = plan.step("vector", [ev_r2], r3)
                tbatch(3, [(fbank[3][0:8, 0:128], pr[:, 12:20])], [ev_r3], idf)
                cmbT_evts.append(
                    evacuate(3, "scalar",
                             (lambda s, _t=tl: s.copy(cmbT[0:8, _t * 128:(_t + 1) * 128],
                                                      fbank[3][0:8, 0:128]))))

            sg_done = []
            for ch in range(2):
                prs = [(gateT[:, ct * 9 + 8: ct * 9 + 9], own_rhs(ct, ch)) for ct in range(CT)]
                chain(fbank[3][0:1, 0:512], 3, prs, [sg_ev, ev_xn_all])
                sg_done.append(
                    evacuate(3, "scalar",
                             (lambda s, _c=ch: s.activation(
                                 cmbrep[32:33, _c * 512:(_c + 1) * 512],
                                 fbank[3][0:1, 0:512], AF.Sigmoid))))

            plan.step("sync", cmbT_evts,
                      lambda g: g.dma_start(out=cmb_scr[0:8, :], in_=cmbT[0:8, :]),
                      kind="dma", sem="sC")
            plan.step("sync", sg_done,
                      lambda g: g.dma_start(out=cmb_scr[8:9, :], in_=sgrow),
                      kind="dma", sem="sC")
            ev_cmb_s = ev_gate_s = plan.now("sC")

            dump("cmbT", cmbT[0:8, :], cmbT_evts)
            dump("kT", kT, [plan.now("sA")])
            dump("vS", vS, [plan.now("sA")])
            dbg_q_ev = plan.now("sD") if dbg else None

            if True:
                def zb_fn(v):
                    v.memset(pb4.bitcast(F32)[:, :], 0.0)
                    v.drain()
                    v.memset(pb5.bitcast(F32)[:, :], 0.0)
                zb = plan.step("vector", banks.wf(4) + banks.wf(5), zb_fn)
                banks.free[4] = zb
                banks.free[5] = zb
            mask_readers = []
            A_par = [cmbrep[:, 0:1024], h1s[:, 0:1024], xio[:, 0:1024]]
            B_par = [bscr, h2c.bitcast(BF16), pbuf[:, 0:2048]]
            S_par = [(stats, 8), (stats, 0), (pr, 0)]
            ln_done_a = plan.now("sA")

            def em_scores(g, hp, j):
                evs = []
                for hh in range(2):
                    h = 2 * hp + hh
                    kvh = h // 4
                    def sc_fn(t, _h=h, _kvh=kvh, _j=j, _hh=hh):
                        q_ap = qao[:, _h * TSC + _j * 128: _h * TSC + (_j + 1) * 128]
                        for b3 in range(3):
                            t.matmul(ps67[:, _hh * 512 + b3 * 128: _hh * 512 + (b3 + 1) * 128],
                                     q_ap,
                                     kT[:, _kvh * KV + (_j + b3) * 128:
                                        _kvh * KV + (_j + b3 + 1) * 128],
                                     start=True, stop=True)
                        return t.matmul(ps67[:, _hh * 512 + 384: _hh * 512 + 388],
                                        q_ap, ksT[:, _kvh * 4: _kvh * 4 + 4],
                                        start=True, stop=True)
                    w = list(banks.wf(6 if hh == 0 else 7)) + [q_ready[hp]]
                    if dbg_q_ev:
                        w.append(dbg_q_ev)
                    cev = plan.step("tensor", w, sc_fn, kind="last")
                    banks.wrote[6 if hh == 0 else 7] = cev
                    evs.append(cev)
                return evs

            def em_m(g, scores_evts, mld_ev):
                par3 = g % 3
                Ap = A_par[par3]
                stt, so = S_par[par3]
                def m_fn(v, _A=Ap, _st=stt, _s=so):
                    srcv = ps67.rearrange("p (h k) -> p h k", h=2)[:, :, 0:388]
                    msk = maskS[:, 0:388].rearrange("p (o k) -> p o k", o=1) \
                        .broadcast_to([128, 2, 388])
                    dst = _A.rearrange("p (h k) -> p h k", h=2)[:, :, 0:388]
                    v.tensor_tensor(dst, srcv, msk, ALU.add)
                    v.drain()
                    v.reduce_max(_st[:, _s:_s + 2], dst, axis=AX.X, negate=True)
                    v.drain()
                    v.tensor_scalar_mul(_st[:, _s + 2:_s + 4], _st[:, _s:_s + 2], SCALE)
                w = scores_evts + [mld_ev, ev_cmb_s]
                if par3 == 2:
                    w.append(ln_done_a)
                ev_m = plan.step("vector", w, m_fn)
                mask_readers.append(ev_m)
                banks.free[6] = ev_m
                banks.free[7] = ev_m
                return ev_m

            def em_e(g, ev_m):
                par3 = g % 3
                Ap = A_par[par3]
                stt, so = S_par[par3]
                def e_fn(s, _A=Ap, _st=stt, _s=so):
                    for hh in range(2):
                        s.activation(_A[:, hh * 512: hh * 512 + 388],
                                     _A[:, hh * 512: hh * 512 + 388], AF.Exp,
                                     bias=_st[:, _s + 2 + hh:_s + 3 + hh],
                                     scale=consts[:, 1:2],
                                     accum_out=_st[:, _s + 4 + hh:_s + 5 + hh])
                        if hh == 0:
                            s.drain()
                return plan.step("scalar", [ev_m], e_fn)

            def em_rn(g, ev_e):
                par3 = g % 3
                Ap = A_par[par3]
                bp = B_par[par3]
                stt, so = S_par[par3]
                def rn_fn(v, _A=Ap, _b=bp, _st=stt, _s=so):
                    v.reciprocal(_st[:, _s + 6:_s + 8], _st[:, _s + 4:_s + 6])
                    v.drain()
                    for hh in range(2):
                        v.tensor_scalar(_b[:, hh * 512: hh * 512 + 388],
                                        _A[:, hh * 512: hh * 512 + 388],
                                        _st[:, _s + 6 + hh:_s + 7 + hh], None, ALU.mult)
                        if hh == 0:
                            v.drain()
                return plan.step("vector", [ev_e], rn_fn)

            def em_tail(g, hp, j, ev_n):
                par2 = g % 2
                par3 = g % 3
                bp = B_par[par3]
                pb, bk = (pb4, 4) if par2 == 0 else (pb5, 5)
                prs = []
                for hh in range(2):
                    for b3 in range(3):
                        prs.append((pb[:, hh * 512 + b3 * 128: hh * 512 + (b3 + 1) * 128],
                                    bp[:, hh * 512 + b3 * 128: hh * 512 + (b3 + 1) * 128]))
                    prs.append((pb[0:4, hh * 512 + 384: (hh + 1) * 512],
                                bp[:, hh * 512 + 384: hh * 512 + 388]))
                tbatch(bk, prs, [ev_n], idb)
                at_ev = evacuate(bk, "scalar",
                                 (lambda s, _pb=pb, _b=bp:
                                  s.copy(_b[:, 1024:2048], _pb[:, 0:1024])))
                def av_fn(t, _hp=hp, _j=j, _b=bp):
                    last = None
                    for hh in range(2):
                        h = 2 * _hp + hh
                        kvh = h // 4
                        o = fbank[3][:, hh * 128:(hh + 1) * 128]
                        for b3 in range(3):
                            last = t.matmul(o,
                                            vS[:, (_j + b3) * 512 + kvh * 128:
                                               (_j + b3) * 512 + (kvh + 1) * 128],
                                            _b[:, 1024 + hh * 512 + b3 * 128:
                                               1024 + hh * 512 + (b3 + 1) * 128],
                                            start=(b3 == 0), stop=False)
                        last = t.matmul(o, vsS[0:4, kvh * 128:(kvh + 1) * 128],
                                        _b[0:4, 1024 + hh * 512 + 384:
                                           1024 + (hh + 1) * 512],
                                        start=False, stop=True)
                    return last
                avw = [at_ev] + banks.wf(3)
                av_ev = plan.step("tensor", avw, av_fn, kind="last")
                banks.wrote[3] = av_ev
                def ao_fn(s, _hp=hp, _j=j):
                    h0 = 2 * _hp
                    s.copy(qao[:, h0 * TSC + _j * 128: h0 * TSC + _j * 128 + 128],
                           fbank[3][:, 0:128])
                    s.drain()
                    s.copy(qao[:, (h0 + 1) * TSC + _j * 128:
                               (h0 + 1) * TSC + _j * 128 + 128],
                           fbank[3][:, 128:256])
                ao_ev = plan.step("scalar", [av_ev], ao_fn)
                banks.free[3] = ao_ev

            q_ready = {}

            def emit_wq_slice(s8):
                slot, rdy, idx = wslice(wqT[:, s8 * 256:(s8 + 1) * 256], 256, CT)
                units = []
                for ftl in range(2):
                    h = 2 * s8 + ftl
                    for ch in range(2):
                        bk = KVROT[(ftl * 2 + ch) % 3]
                        def u(_b=bk, _h=h, _c=ch, _slot=slot, _ftl=ftl, _rdy=rdy):
                            prs = [(wtile(_slot, ct, 256, _ftl), own_rhs(ct, _c))
                                   for ct in range(CT)]
                            ev = chain(fbank[_b], _b, prs, [_rdy, ev_xn_ch[_c]])
                            evacuate(_b, "scalar",
                                     (lambda s, _bb=_b, _hh=_h, _cc=_c:
                                      s.copy(qao[:, _hh * TSC + _cc * 512:
                                                 _hh * TSC + (_cc + 1) * 512],
                                             fbank[_bb])))
                            return ev
                        units.append(u)
                return units, idx

            units0, idx0 = emit_wq_slice(0)
            lastc = None
            for u in units0:
                lastc = u()
            wdone(idx0, lastc)
            q_ready[0] = plan.now("sA")
            pend_units = []
            pend_idx = None

            mld = None
            st_e = {}
            st_rn = {}
            NG = 64
            for g in range(NG):
                hp, j = g // 8, g % 8
                if j == 0 and hp < 7:
                    pend_units, pend_idx = emit_wq_slice(hp + 1)
                    pend_lastc = None
                if j <= 3:
                    mw = list(mask_readers)
                    mld = plan.step("sync", mw,
                                    (lambda gq, _j=j, _sc=sc: gq.dma_start(
                                        out=maskS[:, 0:512], in_=masks[_sc, _j, :, :])),
                                    kind="dma", sem="sM")
                    mask_readers = []
                evs = em_scores(g, hp, j)
                ev_m = em_m(g, evs, mld)
                if pend_units:
                    pend_lastc = pend_units.pop(0)()
                    if not pend_units:
                        wdone(pend_idx, pend_lastc)
                        q_ready[hp + 1] = plan.now("sA")
                if g >= 2:
                    em_tail(g - 2, (g - 2) // 8, (g - 2) % 8, st_rn[g - 2])
                st_e[g] = em_e(g, ev_m)
                if g >= 1:
                    st_rn[g - 1] = em_rn(g - 1, st_e[g - 1])
            st_rn[NG - 1] = em_rn(NG - 1, st_e[NG - 1])
            em_tail(NG - 2, 7, 6, st_rn[NG - 2])
            em_tail(NG - 1, 7, 7, st_rn[NG - 1])

            ao_all = plan.now("sA")
            ao_allA = plan.now("sV")
            attn_done[0] = plan.cnt["sV"]
            attn_done[1] = plan.cnt["sA"]
            av_guard = [plan.cnt["sV"]]
            dump("ao", qao, [ao_all])
            dbg_ao_ev = plan.now("sD") if dbg else None

            out_guard = plan.now("sO")
            wo_last_chain = None
            for s8 in range(8):
                slot, rdy, idx = wslice(woT[:, s8 * 256:(s8 + 1) * 256], 256, CT)
                last_ev = None
                for mt in range(2):
                    ctl = 2 * s8 + mt
                    for ch in range(2):
                        bk = (s8 * 4 + mt * 2 + ch) % 6
                        prs = [(wtile(slot, ct, 256, mt),
                                qao[:, ct * TSC + ch * 512: ct * TSC + (ch + 1) * 512])
                               for ct in range(CT)]
                        w = [rdy, ao_all, ao_allA] + ([dbg_ao_ev] if dbg_ao_ev else [])
                        last_ev = chain(fbank[bk], bk, prs, w)
                        evacuate(bk, "vector",
                                 (lambda v, _b=bk, _c=ctl, _ch=ch:
                                  v.tensor_copy(
                                      acc[:, _c * TSC + _ch * 512: _c * TSC + (_ch + 1) * 512],
                                      fbank[_b])),
                                 extra_waits=[out_guard])
                wo_last_chain = last_ev
                wdone(idx, last_ev)

            dump("acc_wo", acc, [plan.now("sV")])
            dbg_accwo_ev = plan.now("sD") if dbg else None

            state = {"prev_h2c": None, "dbg_h": dbg_accwo_ev}
            state["wo_guard"] = wo_last_chain
            h_slot_ev = [None, None]

            def h_ap2(pe, ft):
                if pe % 2 == 0:
                    return hT[:, ft * TSC: (ft + 1) * TSC]
                if ft < 5:
                    return kT[:, ft * TSC: (ft + 1) * TSC]
                if ft < 10:
                    return vS[:, (ft - 5) * TSC: (ft - 4) * TSC]
                return qao[:, NFT * TSC: NFT * TSC + TSC]

            def h_ap(pe, ft, ch):
                if pe % 2 == 0:
                    return hT[:, ft * TSC + ch * 512: ft * TSC + (ch + 1) * 512]
                if ft < 5:
                    return kT[:, ft * TSC + ch * 512: ft * TSC + (ch + 1) * 512]
                if ft < 10:
                    return vS[:, (ft - 5) * TSC + ch * 512: (ft - 5) * TSC + (ch + 1) * 512]
                return qao[:, NFT * TSC + ch * 512: NFT * TSC + (ch + 1) * 512]

            def em_rep(pe):
                shared = pe >= E
                rep_src = cmb_scr[(8 if shared else pe):(9 if shared else pe + 1), :] \
                    .partition_broadcast(128)
                rw = [ev_cmb_s, ao_all, ("sV", av_guard[0])]
                if state["prev_h2c"] is not None:
                    rw.append(state["prev_h2c"])
                return plan.step("sync", rw,
                                 (lambda g, _s=rep_src: g.dma_start(out=cmbrep, in_=_s)),
                                 kind="dma", sem="sR")

            def em_fc_slice(pe, si, fbase, wdt, hp_i0, ev_rep, h_slot_ev, h_evts):
                shared = pe >= E
                k4 = pe - E
                if shared:
                    s1h = sf1T[:, k4 * FMOE + fbase: k4 * FMOE + fbase + wdt]
                    s2h = sf2T[:, k4 * FMOE + fbase: k4 * FMOE + fbase + wdt]
                else:
                    s1h = fc1T[pe, :, fbase: fbase + wdt]
                    s2h = fc2T[pe, :, fbase: fbase + wdt]
                slot1, rdy1, idx1 = wslice(s1h, wdt, CT)
                slot2, rdy2, idx2 = wslice(s2h, wdt, CT)
                last1 = last2 = None
                hp_i = hp_i0
                for ftl in range(wdt // 128):
                    fglob = (fbase // 128) + ftl
                    evs12 = []
                    for ch in range(2):
                        par = hp_i % 3
                        b1, b2 = [(0, 1), (2, 3), (4, 5)][par]
                        sl = ch
                        prs1 = [(wtile(slot1, ct, wdt, ftl), own_rhs(ct, ch))
                                for ct in range(CT)]
                        last1 = chain(fbank[b1], b1, prs1, [rdy1, ev_xn_ch[ch]])
                        slot_guard = [h_slot_ev[0]] if h_slot_ev[0] else []
                        ev_h1 = evacuate(b1, "scalar",
                                         (lambda s, _b=b1, _sl=sl:
                                          s.activation(h1s[:, _sl * 512:(_sl + 1) * 512],
                                                       fbank[_b], AF.Silu)),
                                         extra_waits=slot_guard)
                        prs2 = [(wtile(slot2, ct, wdt, ftl), own_rhs(ct, ch))
                                for ct in range(CT)]
                        last2 = chain(fbank[b2], b2, prs2, [rdy2, ev_xn_ch[ch]])
                        ev_h2 = evacuate(b2, "vector",
                                         (lambda v, _b=b2, _sl=sl, _ch=ch:
                                          v.tensor_tensor(h2c[:, _sl * 512:(_sl + 1) * 512],
                                                          fbank[_b],
                                                          cmbrep[:, _ch * 512:(_ch + 1) * 512],
                                                          ALU.mult)),
                                         extra_waits=[ev_rep] + slot_guard)
                        state["prev_h2c"] = ev_h2
                        evs12 += [ev_h1, ev_h2]
                        hp_i += 1
                    hw = evs12
                    if state.get("wo_guard") is not None:
                        hw.append(state["wo_guard"])
                    if state["dbg_h"]:
                        hw.append(state["dbg_h"])
                    hstep = plan.step(
                        "vector", hw,
                        (lambda v, _pe=pe, _f=fglob:
                         v.tensor_tensor(
                             h_ap2(_pe, _f),
                             h1s[:, 0:1024],
                             h2c.bitcast(BF16)[:, 0:1024] if False else h2c[:, 0:1024],
                             ALU.mult)))
                    h_evts.append(hstep)
                    h_slot_ev[0] = hstep
                wdone(idx1, last1)
                wdone(idx2, last2)
                return hp_i

            pq = {"i": 0, "done": {}}

            def em_proj_slice(pe, s16, ev_h_all):
                shared = pe >= E
                k4 = pe - E
                if shared:
                    ph = sprT[k4 * FMOE:(k4 + 1) * FMOE, s16 * 128:(s16 + 1) * 128]
                else:
                    ph = prjT[pe, :, s16 * 128:(s16 + 1) * 128]
                i = pq["i"]
                slot = i % 2
                pw = [pq["done"][i - 2]] if i >= 2 else []
                pw.append(ao_all)
                src3 = ph.rearrange("(t p) f -> p t f", p=128)
                dst3 = pbuf[:, slot * 1408:(slot + 1) * 1408] \
                    .rearrange("p (t f) -> p t f", t=NFT)
                rdy = plan.step("gpsimd", pw,
                                (lambda g, _d=dst3, _s=src3: g.dma_start(out=_d, in_=_s)),
                                kind="dma", sem=("sP0" if slot == 0 else "sP1"))
                pq["i"] = i + 1
                last_ev = None
                for ch in range(2):
                    bk = 6 if ch == 0 else 7
                    prs = [(pbuf[:, slot * 1408 + ft * 128:(slot * 1408 + (ft + 1) * 128)],
                            h_ap(pe, ft, ch))
                           for ft in range(NFT)]
                    last_ev = chain(fbank[bk], bk, prs, [rdy, ev_h_all])
                    evacuate(bk, "vector",
                             (lambda v, _b=bk, _c=s16, _ch=ch:
                              v.tensor_tensor(
                                  acc[:, _c * TSC + _ch * 512: _c * TSC + (_ch + 1) * 512],
                                  fbank[_b],
                                  acc[:, _c * TSC + _ch * 512: _c * TSC + (_ch + 1) * 512],
                                  ALU.add)))
                pq["done"][i] = last_ev
                return last_ev

            FC_BASES = []
            fb_ = 0
            for w_ in FC_SLICES:
                FC_BASES.append(fb_)
                fb_ += w_
            prev_hall = None
            state["dbg_h"] = None
            for pe in range(E + 4 + 1):
                do_fc = pe < E + 4
                if (not do_fc) and sc + 1 < n_sc and (sc + 1) not in ln_results:
                    emit_ln(sc + 1)
                if do_fc:
                    if pe < E or pe == E:
                        ev_rep = em_rep(pe)
                    h_evts = []
                    hp_i = 0
                nfc = 6 if do_fc else 0
                npj = 16 if pe > 0 else 0
                fc_i, pj_i = 0, 0
                seq = []
                while fc_i < nfc or pj_i < npj:
                    if fc_i < nfc:
                        seq.append(("fc", fc_i)); fc_i += 1
                    want = npj if fc_i >= nfc else (fc_i * npj + nfc - 1) // max(nfc, 1)
                    while pj_i < min(want, npj):
                        seq.append(("pj", pj_i)); pj_i += 1
                for kind, i in seq:
                    if kind == "fc":
                        hp_i = em_fc_slice(pe, i, FC_BASES[i], FC_SLICES[i],
                                           hp_i, ev_rep, h_slot_ev, h_evts)
                    else:
                        em_proj_slice(pe - 1, i, prev_hall)
                if do_fc:
                    prev_hall = ("sV", max(v for (_, v) in h_evts))
                    if pe == 0:
                        dump("h0", hT, [prev_hall])
                        state["dbg_h"] = plan.now("sD") if dbg else None

            dump("acc_fin", acc, [plan.now("sV")])

            acc_ready = plan.now("sV")
            prev_xa = None
            for tl in range(NQT):
                tok = tl * 128
                dma_waits = []
                for q4 in range(4):
                    bk = q4 % 4
                    pbk = fbank[bk]
                    prs = [(pbk[:, i * 128:(i + 1) * 128],
                            acc[:, (q4 * 4 + i) * TSC + tok: (q4 * 4 + i) * TSC + tok + 128])
                           for i in range(4)]
                    tbatch(bk, prs, [acc_ready], idf)
                    dma_waits.append(
                        evacuate(bk, "scalar",
                                 (lambda s, _q=q4, _b=bk, _tok=tok:
                                  s.copy(acc3[:, _q * 4:(_q + 1) * 4, _tok:_tok + 128],
                                         fbank[_b].rearrange("p (i r) -> p i r", i=4)))))
                xsrc = x_own[g0 + tok: g0 + tok + 128, :].rearrange("p (t r) -> p t r", t=CT)
                xw = list(dma_waits)
                if prev_xa is not None:
                    xw.append(prev_xa)
                xaev = plan.step("gpsimd", xw,
                                 (lambda g, _s=xsrc, _tok=tok: g.dma_start(
                                     out=acc3[:, :, _tok:_tok + 128], in_=_s,
                                     accum_op=ALU.add)),
                                 kind="dma", sem="sXA")
                prev_xa = xaev
                oev = plan.step("sync", [xaev],
                                (lambda g, _t=tl, _g0=g0, _tok=tok: g.dma_start(
                                    out=out[_g0 + _t * 128: _g0 + (_t + 1) * 128, :]
                                    .rearrange("p (t r) -> p t r", t=CT),
                                    in_=acc3[:, :, _tok:_tok + 128])),
                                kind="dma", sem="sO")
                last_out_ev = oev

        plan.step("sync", [last_out_ev], lambda g: g.nop(), kind="none")

        block.sync(lambda e: plan.run("sync", e, sems))
        block.gpsimd(lambda e: plan.run("gpsimd", e, sems))
        block.vector(lambda e: plan.run("vector", e, sems))
        block.scalar(lambda e: plan.run("scalar", e, sems))
        block.tensor(lambda e: plan.run("tensor", e, sems))

    return nc, dbg_outs


def make_masks(half):
    m = np.full((NSC, 4, 128, 512), NEG, dtype=np.float32)
    ql = np.arange(128)
    for sc in range(NSC):
        for j in range(4):
            gq = half * 16 + sc * 8 + j
            q = 128 * gq + ql[:, None]
            for b3 in range(3):
                kt = gq + b3 - 2
                if kt < 0:
                    continue
                k = 128 * kt + np.arange(128)[None, :]
                valid = (k <= q) & ((k >= q - 255) | (k < 4))
                blk = m[sc, j, :, b3 * 128:(b3 + 1) * 128]
                blk[valid] = 0.0
            if gq >= 3:
                k = np.arange(4)[None, :]
                valid = np.broadcast_to(k <= q, (128, 4))
                blk = m[sc, j, :, 384:388]
                blk[valid] = 0.0
    return m


_NC_CACHE = {}


def kernel(L0_attn_x, L0_attn_wq, L0_attn_wk, L0_attn_wv, L0_attn_wo,
           L0_ffn_gate_w, L0_ffn_expert_fc1, L0_ffn_expert_fc2, L0_ffn_expert_proj,
           L0_ffn_shared_fc1, L0_ffn_shared_fc2, L0_ffn_shared_proj,
           L0_ffn_shared_gate_w):
    x = np.asarray(L0_attn_x, dtype=np.float32)
    tp = lambda w: np.ascontiguousarray(np.asarray(w, dtype=np.float32).T)
    wqT_, wkT_, wvT_, woT_ = tp(L0_attn_wq), tp(L0_attn_wk), tp(L0_attn_wv), tp(L0_attn_wo)
    gT = np.ascontiguousarray(np.concatenate(
        [np.asarray(L0_ffn_gate_w, dtype=np.float32).T,
         np.asarray(L0_ffn_shared_gate_w, dtype=np.float32).T], axis=1))
    f1T = np.ascontiguousarray(np.asarray(L0_ffn_expert_fc1, dtype=np.float32).transpose(0, 2, 1))
    f2T = np.ascontiguousarray(np.asarray(L0_ffn_expert_fc2, dtype=np.float32).transpose(0, 2, 1))
    pjT = np.ascontiguousarray(np.asarray(L0_ffn_expert_proj, dtype=np.float32).transpose(0, 2, 1))
    s1T, s2T, spT = tp(L0_ffn_shared_fc1), tp(L0_ffn_shared_fc2), tp(L0_ffn_shared_proj)

    if "nc" not in _NC_CACHE:
        _NC_CACHE["nc"] = build(dbg=0)[0]
    nc = _NC_CACHE["nc"]

    zeros_halo = np.zeros((HALO, D), dtype=np.float32)
    masks_by_half = [make_masks(0), make_masks(1)]
    in_maps = []
    for c in range(8):
        b, half = c // 2, c % 2
        in_maps.append({
            "x_own": np.ascontiguousarray(x[b, half * 2048:(half + 1) * 2048]),
            "x_halo": zeros_halo if half == 0 else np.ascontiguousarray(x[b, 1792:2048]),
            "x_sink": np.ascontiguousarray(x[b, 0:4]),
            "wqT": wqT_, "wkT": wkT_, "wvT": wvT_, "woT": woT_,
            "gateT": gT,
            "efc1T": f1T, "efc2T": f2T, "eprjT": pjT,
            "sfc1T": s1T, "sfc2T": s2T, "sprjT": spT,
            "masks": masks_by_half[half],
        })
    res = run_bass_kernel_spmd(nc, in_maps, core_ids=list(range(8)))
    outp = np.empty((4, 4096, D), dtype=np.float32)
    for c in range(8):
        b, half = c // 2, c % 2
        outp[b, half * 2048:(half + 1) * 2048] = res.results[c]["out"]
    return outp
